# version 05 (final)
# baseline (speedup 1.0000x reference)
import numpy as np
import ml_dtypes
import concourse.bass as bass
import concourse.mybir as mybir
from concourse.bass_utils import run_bass_kernel_spmd

F32 = mybir.dt.float32
BF16 = mybir.dt.bfloat16
U16 = mybir.dt.uint16
AF = mybir.ActivationFunctionType
ALU = mybir.AluOpType
AX = mybir.AxisListType

B, T, D = 2, 4096, 2048
NCORE, GRP = 8, 4
TOK = 1024
I_FFN = 8192
GATE_OFFSET = 6.906768
C_RNE = 1.5 * 2.0 ** 23
SCALE_QK = float(1.0 / np.sqrt(np.float64(128.0)))
NST = T // 128

STAGE = 99
SERIALIZE_CC = True
SUB6 = 9


def ts(i, w=128):
    return slice(i * w, (i + 1) * w)


class KB:
    def __init__(self):
        nc = bass.Bass("TRN2", target_bir_lowering=False)
        self.nc = nc
        inp = {}

        def I(name, shape, dt=F32):
            inp[name] = nc.dram_tensor(name, shape, dt, kind="ExternalInput")
        self.inp = inp
        I("x_fm", [16, 128, TOK])
        I("mask01", [NST, 128, TOK], BF16)
        I("selhot", [NST, 128, TOK], BF16)
        I("rope_cos", [128, 8, 64])
        I("rope_sin", [128, 8, 64])
        I("b1_r", [128, 64]); I("b2_r", [128, 16]); I("bc_r", [128, 16])
        I("L0_Wq", [D, 2048]); I("L0_Wk", [D, 1024]); I("L0_Wv", [D, 1024])
        I("L0_Wg", [D, 8]); I("L0_Wc", [2048, D])
        I("w1_shard", [1024, D]); I("w2_shard", [256, I_FFN])
        I("L1_Wq", [2048, D]); I("L1_Wk", [512, D]); I("L1_Wv", [512, D])
        I("L1_Wp", [D, 2048])
        I("gate_shard", [1024, D]); I("up_shard", [1024, D])
        I("down_shard", [256, I_FFN])
        self.out_fm = nc.dram_tensor("out_fm", [16, 128, TOK], F32,
                                     kind="ExternalOutput")
        self.dbg = {}

        dr = {}

        def DR(name, shape, dt):
            dr[name] = nc.dram_tensor(name, shape, dt)
        self.dr = dr
        DR("wq0_b", [D, 2048], BF16); DR("wk0_b", [D, 1024], BF16)
        DR("wv0_b", [D, 1024], BF16); DR("wg0_b", [D, 8], BF16)
        DR("wc0_b", [2048, D], BF16)
        DR("wq1_b", [2048, D], BF16); DR("wk1_b", [512, D], BF16)
        DR("wv1_b", [512, D], BF16); DR("wp1_b", [D, 2048], BF16)
        DR("qw1t_in", [D, 1024], BF16); DR("qw1t_ag", [NCORE, D, 1024], BF16)
        DR("sw1_in", [1, 1024], F32); DR("sw1_ag", [NCORE, 1, 1024], F32)
        DR("qw2t_in", [I_FFN, 256], BF16); DR("qw2t_ag", [NCORE, I_FFN, 256], BF16)
        DR("sw2_in", [1, 256], F32); DR("sw2_ag", [NCORE, 1, 256], F32)
        DR("dqgt_in", [D, 1024], BF16); DR("dqgt_ag", [NCORE, D, 1024], BF16)
        DR("dqut_in", [D, 1024], BF16); DR("dqut_ag", [NCORE, D, 1024], BF16)
        DR("dqdt_in", [I_FFN, 256], BF16); DR("dqdt_ag", [NCORE, I_FFN, 256], BF16)
        DR("k0_in", [8, 128, TOK], BF16); DR("k0_ag", [GRP, 8, 128, TOK], BF16)
        DR("v0_in", [TOK, 1024], BF16); DR("v0_ag", [GRP, TOK, 1024], BF16)
        DR("g0_in", [8, TOK], F32); DR("g0_ag", [GRP, 8, TOK], F32)
        DR("k1_in", [TOK, 512], BF16); DR("k1_ag", [GRP, TOK, 512], BF16)
        DR("v1_in", [TOK, 512], BF16); DR("v1_ag", [GRP, TOK, 512], BF16)
        DR("mx1_in", [1, 128], F32); DR("mx1_out", [1, 128], F32)
        DR("mx2_in", [1, 128], F32); DR("mx2_out", [1, 128], F32)
        DR("flat1", [1, 4096], F32)
        DR("fpl_d", [8, 1024], F32)
        DR("emt_dbg", [128, 384], F32)
        DR("h0_d", [64, 128, TOK], F32)
        DR("hq_d", [64, 128, TOK], BF16)
        DR("qxn_d", [16, 128, TOK], BF16)
        DR("q0_d", [16, 128, TOK], BF16)
        DR("y_d", [16, 128, TOK], BF16)
        DR("xn1b_d", [16, 128, TOK], BF16)
        DR("dqx1_d", [TOK, D], BF16)
        DR("q1_d", [TOK, 2048], BF16)
        DR("h2_d", [TOK, I_FFN], BF16)
        DR("x1_d", [16, 128, TOK], F32)

        A = nc.alloc_sbuf_tensor
        self.AR0 = A("AR0", [128, 16384], F32)
        self.RES = A("RES", [128, 16, TOK], F32)
        self.AB0 = A("AB0", [128, 16384], BF16)
        sm = {}

        def S(name, shape, dt=F32):
            sm[name] = A(name, shape, dt)
        self.sm = sm
        S("ktile", [128, 2, 128], BF16)
        S("vtile", [128, 2, 128], BF16)
        S("mtile", [128, 2, 512], BF16)
        S("atile", [128, 2, 512], BF16)
        S("qslice", [128, 2, 512], BF16)
        S("wld", [128, 2, 8, 128], BF16)
        S("rhld", [128, 2, 1024], BF16)
        S("ev", [128, 1024])
        S("ev2", [128, 1024])
        S("hq", [128, 1024], BF16)
        S("row", [1, 1024])
        S("row2", [1, 1024])
        S("stat", [128, 64])
        S("scal", [128, 32])
        S("sxw1", [128, 64]); S("b1s", [128, 64]); S("sw1f", [128, 64])
        S("sxw2", [128, 16]); S("b2s", [128, 16]); S("bcs", [128, 16])
        S("sw2f", [128, 16])
        S("identb", [128, 128], BF16)
        S("identf", [128, 128], F32)
        S("onesb", [128, 8], BF16)
        S("onesf", [128, 8], F32)
        S("mhat", [128, 16])
        S("bcast", [128, 1024])
        S("bcast2", [128, 1024])
        S("ropc", [128, 8, 64], BF16); S("rops", [128, 8, 64], BF16)
        S("wps", [128, 8]); S("wpr", [128, 8])
        self.ps = [nc.alloc_psum_tensor(f"ps{i}", [128, 512], F32) for i in range(8)]

        self.wprep = self.AR0[:].rearrange("p (a k) -> p a k", a=8)
        self.wprep_q = self.AB0[:].rearrange("p (a k) -> p a k", a=8)
        self.xnb = self.AB0[:].rearrange("p (c t) -> p c t", c=16)
        self.pf1 = self.wprep[:, 4, :]
        self.pf2 = self.wprep[:, 5, :]
        self.peb = self.wprep[:, 6, 0:1024].bitcast(U16)
        self.tpst = self.wprep[:, 7, 0:1024].bitcast(BF16).rearrange(
            "p (n r) -> p n r", n=16)
        self.g_sb = self.AR0[0:8, 0:4096]
        self.u_sb = self.AR0[0:8, 4096:8192]
        self.acc_sb = self.AR0[0:8, 8192:12288]
        self.lg_sb = self.AR0[0:8, 12288:16384]
        self.em_sb = self.AR0[0:8, 0:4096]
        self.emt = self.AR0[:, 4096:4352].rearrange("p (s h) -> p s h", h=8)
        self.emtb = self.AR0[:, 4352:4480].bitcast(BF16).rearrange(
            "p (s h) -> p s h", h=8)
        self.sems = {}
        self.cc_counts = {}
        self.persist = ("cc_w", "sc1_cc", "sc2_cc", "cc_kv0", "cc_kv1")
        for nm in self.persist:
            self.sem(nm)

    def sem(self, name):
        if name not in self.sems:
            self.sems[name] = self.nc.alloc_semaphore(name)
        return self.sems[name]

    def free_phase_sems(self):
        drop = [k for k in self.sems if k not in self.persist]
        if not drop:
            return
        self.nc.clear_and_free_semaphores([self.sems[k] for k in drop])
        self.nc.all_engine_barrier()
        for k in drop:
            del self.sems[k]

    def dbg_out(self, name, shape, dt=F32):
        if name not in self.dbg:
            self.dbg[name] = self.nc.dram_tensor("dbg_" + name, shape, dt,
                                                 kind="ExternalOutput")
        return self.dbg[name]

    def v_mx_dq(self, v, src_ap, out_ap, K, f1, f2, eb, bm, sc, rs):
        nb = K // 32
        v.tensor_reduce(bm, src_ap.rearrange("p (n b) -> p n b", b=32),
                        axis=AX.X, op=ALU.max, apply_absolute_value=True)
        v.drain()
        v.tensor_scalar(sc, bm, 1.0 / 448.0, 1e-30, op0=ALU.mult, op1=ALU.max)
        v.drain()
        v.reciprocal(rs, sc)
        v.drain()
        v.tensor_tensor(f1.rearrange("p (n b) -> p n b", b=32),
                        src_ap.rearrange("p (n b) -> p n b", b=32),
                        rs.unsqueeze(2).broadcast_to([128, nb, 32]), op=ALU.mult)
        v.drain()
        vhi = f1.bitcast(U16).rearrange("p (n two) -> p n two", two=2)[:, :, 1]
        v.tensor_scalar(eb, vhi, 0x7F80, None, op0=ALU.bitwise_and)
        v.drain()
        v.tensor_scalar(f2, eb.bitcast(BF16), 0.125, 2.0 ** -9,
                        op0=ALU.mult, op1=ALU.max)
        v.drain()
        v.reciprocal(src_ap, f2)
        v.drain()
        v.tensor_tensor(f1, f1, src_ap, op=ALU.mult)
        v.drain()
        v.tensor_scalar(f1, f1, C_RNE, C_RNE, op0=ALU.add, op1=ALU.subtract)
        v.drain()
        v.tensor_tensor(f1, f1, f2, op=ALU.mult)
        v.drain()
        v.tensor_tensor(out_ap.rearrange("p (n b) -> p n b", b=32),
                        f1.rearrange("p (n b) -> p n b", b=32),
                        sc.unsqueeze(2).broadcast_to([128, nb, 32]), op=ALU.mult)
        v.drain()

    def p0(self):
        nc, inp, dr, sm = self.nc, self.inp, self.dr, self.sm
        s_d = self.sem("p0d")
        with nc.Block() as blk:
            @blk.sync
            def _(sy):
                for nm, dst in [("b1_r", "b1s"), ("b2_r", "b2s"), ("bc_r", "bcs")]:
                    sy.dma_start(sm[dst][:], inp[nm][:]).then_inc(s_d, 16)

            @blk.gpsimd
            def _(g):
                g.dma_start(out=sm["ropc"][:], in_=inp["rope_cos"][:]).then_inc(s_d, 16)
                g.dma_start(out=sm["rops"][:], in_=inp["rope_sin"][:]).then_inc(s_d, 16)
                for src, dst in [("L0_Wq", "wq0_b"), ("L0_Wk", "wk0_b"),
                                 ("L0_Wv", "wv0_b"), ("L0_Wg", "wg0_b"),
                                 ("L0_Wc", "wc0_b"), ("L1_Wq", "wq1_b"),
                                 ("L1_Wk", "wk1_b"), ("L1_Wv", "wv1_b"),
                                 ("L1_Wp", "wp1_b")]:
                    g.dma_start(out=dr[dst][:], in_=inp[src][:]).then_inc(s_d, 16)
                g.memset(sm["identb"][:], 0.0)
                g.affine_select(out=sm["identb"][:], in_=sm["identb"][:],
                                compare_op=ALU.not_equal, fill=1.0, base=0,
                                pattern=[[-1, 128]], channel_multiplier=1)
                g.memset(sm["identf"][:], 0.0)
                g.affine_select(out=sm["identf"][:], in_=sm["identf"][:],
                                compare_op=ALU.not_equal, fill=1.0, base=0,
                                pattern=[[-1, 128]], channel_multiplier=1)

            @blk.vector
            def _(v):
                v.memset(sm["onesb"][:], 1.0)
                v.memset(sm["onesf"][:], 1.0)
                v.wait_ge(s_d, 16 * 14)

    def _prep_shard(self, shard, nrows, K, qt_name, s_name=None):
        nc, sm = self.nc, self.sm
        tag = qt_name
        ntile, nseg = nrows // 128, K // 2048
        segs = [(i, j) for i in range(ntile) for j in range(nseg)]
        assert len(segs) == 8
        for ba in range(2):
            batch = segs[4 * ba:4 * ba + 4]
            s_l = self.sem(tag + f"_l{ba}")
            s_p = self.sem(tag + f"_p{ba}")
            s_c = self.sem(tag + f"_c{ba}")
            s_s = self.sem(tag + f"_s{ba}")
            s_d = self.sem(tag + f"_d{ba}")
            with nc.Block() as blk:
                @blk.sync
                def _(sy):
                    for si, (i, j) in enumerate(batch):
                        sy.dma_start(self.wprep[:, si, :],
                                     self.inp[shard][ts(i), ts(j, 2048)]
                                     ).then_inc(s_l, 16)

                @blk.vector
                def _(v):
                    v.wait_ge(s_l, 64)
                    if s_name is not None:
                        rowtiles = sorted(set(i for i, j in batch))
                        if nseg == 1:
                            for si, (i, j) in enumerate(batch):
                                v.tensor_reduce(sm["wpr"][:, i:i + 1],
                                                self.wprep[:, si, :], axis=AX.X,
                                                op=ALU.max, apply_absolute_value=True)
                            v.drain()
                        else:
                            for si, (i, j) in enumerate(batch):
                                v.tensor_reduce(sm["stat"][:, si:si + 1],
                                                self.wprep[:, si, :], axis=AX.X,
                                                op=ALU.max, apply_absolute_value=True)
                            v.drain()
                            for i in rowtiles:
                                cols = [si for si, (ii, j) in enumerate(batch) if ii == i]
                                v.tensor_reduce(sm["wpr"][:, i:i + 1],
                                                sm["stat"][:, cols[0]:cols[-1] + 1],
                                                axis=AX.X, op=ALU.max)
                            v.drain()
                        for i in rowtiles:
                            v.tensor_scalar(sm["wps"][:, i:i + 1], sm["wpr"][:, i:i + 1],
                                            1.0 / 127.0, 1e-8, op0=ALU.mult, op1=ALU.max)
                        v.drain()
                        for i in rowtiles:
                            v.reciprocal(sm["wpr"][:, i:i + 1], sm["wps"][:, i:i + 1])
                        v.drain()
                        for si, (i, j) in enumerate(batch):
                            v.tensor_scalar(self.wprep[:, si, :], self.wprep[:, si, :],
                                            sm["wpr"][:, i:i + 1], C_RNE,
                                            op0=ALU.mult, op1=ALU.add)
                            v.drain()
                            v.tensor_scalar(self.wprep[:, si, :], self.wprep[:, si, :],
                                            C_RNE, 127.0, op0=ALU.subtract, op1=ALU.min)
                            v.drain()
                            v.tensor_scalar(self.wprep_q[:, si, :], self.wprep[:, si, :],
                                            -127.0, None, op0=ALU.max)
                            v.drain()
                    else:
                        for si in range(4):
                            self.v_mx_dq(v, self.wprep[:, si, :],
                                         self.wprep_q[:, si, :], 2048,
                                         self.pf1, self.pf2, self.peb,
                                         sm["stat"][:, 0:64], sm["bcast"][:, 0:64],
                                         sm["bcast"][:, 64:128])

            with nc.Block() as blk:
                psb = [p.bitcast(BF16)[:, 0:128] for p in self.ps[:2]]

                @blk.tensor
                def _(te):
                    n = 0
                    for si in range(4):
                        for k in range(16):
                            if n >= 2:
                                te.wait_ge(s_c, n - 1)
                            te.transpose(psb[n % 2], self.wprep_q[:, si, ts(k)],
                                         sm["identb"][:]).then_inc(s_p, 1)
                            n += 1

                @blk.vector
                def _(v):
                    n = 0
                    for si in range(4):
                        if si >= 1:
                            v.wait_ge(s_d, 16 * si)
                        for k in range(16):
                            v.wait_ge(s_p, n + 1)
                            v.tensor_copy(self.tpst[:, k, :], psb[n % 2]
                                          ).then_inc(s_c, 1)
                            v.drain()
                            n += 1
                        v.sem_inc(s_s, 1)

                @blk.sync
                def _(sy):
                    for si, (i, j) in enumerate(batch):
                        sy.wait_ge(s_s, si + 1)
                        sy.dma_start(
                            self.dr[qt_name][ts(j, 2048), ts(i)]
                            .rearrange("(n p) r -> p n r", p=128),
                            self.tpst[:]).then_inc(s_d, 16)
                    sy.wait_ge(s_d, 64)
            self.free_phase_sems()
        if s_name is not None:
            s_e = self.sem(tag + "_e")
            with nc.Block() as blk:
                @blk.sync
                def _(sy):
                    sy.dma_start(
                        self.dr[s_name][0:1, :].squeeze(0)
                        .rearrange("(i p) -> p i", p=128)[:, :ntile],
                        sm["wps"][:, :ntile],
                        allow_slow_non_contiguous=True).then_inc(s_e, 16)
                    sy.wait_ge(s_e, 16)

    def p_ln(self, tag, load_x):
        nc, sm = self.nc, self.sm
        s_l = self.sem(tag + "_l")
        s_sq = self.sem(tag + "_sq")
        s_mm = self.sem(tag + "_mm")
        evs = [sm["ev"][:, 0:1024], sm["ev2"][:, 0:1024]]
        with nc.Block() as blk:
            if load_x:
                @blk.sync
                def _(sy):
                    for c in range(16):
                        sy.dma_start(self.RES[:, c, :],
                                     self.inp["x_fm"][c, :, :]).then_inc(s_l, 16)
                    sy.wait_ge(s_l, 256)

            @blk.scalar
            def _(s):
                for c in range(16):
                    if load_x:
                        s.wait_ge(s_l, 16 * (c + 1))
                    if c >= 2:
                        s.wait_ge(s_mm, 4 * (c - 1))
                    s.activation(evs[c % 2], self.RES[:, c, :],
                                 AF.Square).then_inc(s_sq, 1)

            @blk.tensor
            def _(te):
                for c in range(16):
                    if load_x:
                        te.wait_ge(s_l, 16 * (c + 1))
                    te.wait_ge(s_sq, c + 1)
                    for h in range(2):
                        te.matmul(self.ps[h][0:1, :], sm["onesf"][:, 0:1],
                                  self.RES[:, c, ts(h, 512)],
                                  start=(c == 0), stop=(c == 15)).then_inc(s_mm, 1)
                        te.matmul(self.ps[2 + h][0:1, :], sm["onesf"][:, 0:1],
                                  evs[c % 2][:, ts(h, 512)],
                                  start=(c == 0), stop=(c == 15)).then_inc(s_mm, 1)
        with nc.Block() as blk:
            @blk.vector
            def _(v):
                v.memset(sm["scal"][:, 15:16], 1e-5)
                v.drain()
                for h in range(2):
                    v.tensor_scalar(sm["row"][0:1, ts(h, 512)], self.ps[h][0:1, :],
                                    1.0 / D, None, op0=ALU.mult)
                v.drain()
                for h in range(2):
                    v.tensor_tensor(sm["row2"][0:1, ts(h, 512)],
                                    sm["row"][0:1, ts(h, 512)],
                                    sm["row"][0:1, ts(h, 512)], op=ALU.mult)
                v.drain()
                for h in range(2):
                    v.scalar_tensor_tensor(sm["row2"][0:1, ts(h, 512)],
                                           self.ps[2 + h][0:1, :], 1.0 / D,
                                           sm["row2"][0:1, ts(h, 512)],
                                           op0=ALU.mult, op1=ALU.subtract)
                v.drain()
        with nc.Block() as blk:
            @blk.scalar
            def _(s):
                s.activation(sm["row2"][0:1, :], sm["row2"][0:1, :], AF.Sqrt,
                             bias=sm["scal"][0:1, 15:16])
                s.drain()
        with nc.Block() as blk:
            @blk.vector
            def _(v):
                v.reciprocal(sm["row2"][0:1, :], sm["row2"][0:1, :])
                v.drain()
                v.scalar_tensor_tensor(sm["row"][0:1, :], sm["row"][0:1, :], -1.0,
                                       sm["row2"][0:1, :], op0=ALU.mult, op1=ALU.mult)
                v.drain()
        s_b = self.sem(tag + "_b")
        with nc.Block() as blk:
            @blk.sync
            def _(sy):
                sy.dma_start(self.dr["flat1"][0:1, 0:1024],
                             sm["row2"][0:1, :]).then_inc(s_b, 16)
                sy.dma_start(self.dr["flat1"][0:1, 1024:2048],
                             sm["row"][0:1, :]).then_inc(s_b, 16)
                sy.wait_ge(s_b, 32)
                sy.dma_start(sm["bcast"][:, :],
                             self.dr["flat1"][0:1, 0:1024]
                             .broadcast_to([128, 1024])).then_inc(s_b, 16)
                sy.dma_start(sm["bcast2"][:, :],
                             self.dr["flat1"][0:1, 1024:2048]
                             .broadcast_to([128, 1024])).then_inc(s_b, 16)
                sy.wait_ge(s_b, 64)
        with nc.Block() as blk:
            @blk.vector
            def _(v):
                for c in range(16):
                    v.tensor_tensor(sm["ev"][:, 0:1024], self.RES[:, c, :],
                                    sm["bcast"][:, :], op=ALU.mult)
                    v.drain()
                    v.tensor_tensor(sm["ev2"][:, 0:1024], sm["ev"][:, 0:1024],
                                    sm["bcast2"][:, :], op=ALU.add)
                    v.drain()
                    v.tensor_copy(self.xnb[:, c, :], sm["ev2"][:, 0:1024])
                    v.drain()
                    v.tensor_reduce(sm["stat"][:, c:c + 1], sm["ev2"][:, 0:1024],
                                    axis=AX.X, op=ALU.max, apply_absolute_value=True)
                    v.drain()

    def p_scale(self, tag, ncols, mx_in, mx_out, s_col, rs_col, denom):
        nc, sm = self.nc, self.sm
        s_a = self.sem(tag + "_a")
        s_t = self.sem(tag + "_t")
        s_cc = self.sem(tag + "_cc")
        self.cc_counts[tag + "_cc"] = 1
        with nc.Block() as blk:
            @blk.vector
            def _(v):
                v.tensor_reduce(sm["wps"][:, 0:1], sm["stat"][:, 0:ncols],
                                axis=AX.X, op=ALU.max)
                v.drain()
                v.sem_inc(s_a, 1)

            @blk.tensor
            def _(te):
                te.wait_ge(s_a, 1)
                te.transpose(self.ps[6][0:1, 0:128], sm["wps"][:, 0:1],
                             sm["identf"][:]).then_inc(s_t, 1)
        with nc.Block() as blk:
            @blk.vector
            def _(v):
                v.tensor_reduce(sm["row2"][0:1, 128:129], self.ps[6][0:1, 0:128],
                                axis=AX.X, op=ALU.max)
                v.drain()
                v.tensor_scalar(sm["row2"][0:1, 256:264],
                                sm["row2"][0:1, 128:129].broadcast_to([1, 8]),
                                1.0, None, op0=ALU.mult)
                v.drain()
                v.sem_inc(s_a, 1)

            @blk.sync
            def _(sy):
                sy.wait_ge(s_a, 2)
                sy.dma_start(self.dr[mx_in][0:1, 0:8],
                             sm["row2"][0:1, 256:264]).then_inc(s_a, 16)
                sy.wait_ge(s_a, 18)
        with nc.Block() as blk:
            @blk.gpsimd
            def _(g):
                g.collective_compute(
                    "AllReduce", ALU.max, replica_groups=[list(range(NCORE))],
                    ins=[self.dr[mx_in].ap().opt()],
                    outs=[self.dr[mx_out].ap().opt()]).then_inc(s_cc, 1)
        with nc.Block() as blk:
            @blk.sync
            def _(sy):
                sy.wait_ge(s_cc, 1)
                sy.dma_start(sm["row2"][0:1, 264:272],
                             self.dr[mx_out][0:1, 0:8]).then_inc(s_a, 16)
                sy.wait_ge(s_a, 34)

            @blk.vector
            def _(v):
                v.wait_ge(s_a, 34)
                v.tensor_scalar(sm["row2"][0:1, 272:280], sm["row2"][0:1, 264:272],
                                1.0 / denom, 1e-8, op0=ALU.mult, op1=ALU.max)
                v.drain()
                v.reciprocal(sm["row2"][0:1, 280:288], sm["row2"][0:1, 272:280])
                v.drain()
                v.sem_inc(s_a, 1)
        with nc.Block() as blk:
            @blk.sync
            def _(sy):
                sy.dma_start(self.dr["flat1"][0:1, 2048:2064],
                             sm["row2"][0:1, 272:288]).then_inc(s_a, 16)
                sy.wait_ge(s_a, 51)
                sy.dma_start(sm["scal"][:, s_col:s_col + 8],
                             self.dr["flat1"][0:1, 2048:2056]
                             .broadcast_to([128, 8])).then_inc(s_a, 16)
                sy.dma_start(sm["scal"][:, rs_col:rs_col + 8],
                             self.dr["flat1"][0:1, 2056:2064]
                             .broadcast_to([128, 8])).then_inc(s_a, 16)
                sy.wait_ge(s_a, 83)

    def p_qxn(self, tag, rs_col):
        nc, sm = self.nc, self.sm
        s_d = self.sem(tag + "_d")
        with nc.Block() as blk:
            @blk.vector
            def _(v):
                for c in range(16):
                    if c >= 2:
                        v.wait_ge(s_d, 16 * (c - 1))
                    v.tensor_tensor(sm["ev"][:, 0:1024], self.RES[:, c, :],
                                    sm["bcast"][:, :], op=ALU.mult)
                    v.drain()
                    v.tensor_tensor(sm["ev"][:, 0:1024], sm["ev"][:, 0:1024],
                                    sm["bcast2"][:, :], op=ALU.add)
                    v.drain()
                    v.tensor_scalar(sm["ev"][:, 0:1024], sm["ev"][:, 0:1024],
                                    sm["scal"][:, rs_col:rs_col + 1], C_RNE,
                                    op0=ALU.mult, op1=ALU.add)
                    v.drain()
                    v.tensor_scalar(sm["ev"][:, 0:1024], sm["ev"][:, 0:1024],
                                    C_RNE, 127.0, op0=ALU.subtract, op1=ALU.min)
                    v.drain()
                    v.tensor_scalar(sm["hq"][:, 0:1024] if c % 2 == 0 else
                                    sm["rhld"][:, 0, :],
                                    sm["ev"][:, 0:1024], -127.0, None, op0=ALU.max)
                    v.drain()
                    v.sem_inc(self.sem(tag + "_q"), 1)

            @blk.sync
            def _(sy):
                for c in range(16):
                    sy.wait_ge(self.sem(tag + "_q"), c + 1)
                    sy.dma_start(self.dr["qxn_d"][c, :, :],
                                 sm["hq"][:, 0:1024] if c % 2 == 0 else
                                 sm["rhld"][:, 0, :]).then_inc(s_d, 16)
                sy.wait_ge(s_d, 256)

    def p_proj_fm(self, tag, w_dram, dout, out_dram):
        nc, sm = self.nc, self.sm
        ndc = dout // 128
        ngrp = (ndc + 7) // 8
        for grp in range(ngrp):
            j0, j1 = grp * 8, min(ndc, grp * 8 + 8)
            nj = j1 - j0
            for h in range(2):
                self.free_phase_sems()
                tg = f"{tag}{grp}_{h}"
                s_w = self.sem(tg + "w")
                s_mm = self.sem(tg + "m")
                s_ev = self.sem(tg + "e")
                s_st = self.sem(tg + "s")
                with nc.Block() as blk:
                    @blk.sync
                    def _(sy):
                        for k in range(16):
                            if k >= 2:
                                sy.wait_ge(s_mm, k - 1)
                            sy.dma_start(
                                sm["wld"][:, k % 2, 0:nj, :],
                                self.dr[w_dram][ts(k), j0 * 128:j1 * 128]
                                .rearrange("p (j q) -> p j q", q=128)
                            ).then_inc(s_w, 16)
                        sy.wait_ge(s_w, 16 * 16)

                    @blk.tensor
                    def _(te):
                        for k in range(16):
                            te.wait_ge(s_w, 16 * (k + 1))
                            for j in range(nj):
                                mm = te.matmul(self.ps[j][:, :],
                                               sm["wld"][:, k % 2, j, :],
                                               self.xnb[:, k, ts(h, 512)],
                                               start=(k == 0), stop=(k == 15))
                            mm.then_inc(s_mm, 1)

                    @blk.scalar
                    def _(s):
                        s.wait_ge(s_mm, 16)
                        for j in range(nj):
                            if j >= 2:
                                s.wait_ge(s_st, 16 * (j - 1))
                            s.activation(sm["hq"][:, ts(j % 2, 512)],
                                         self.ps[j][:, :],
                                         AF.Identity).then_inc(s_ev, 1)

                    @blk.gpsimd
                    def _(g):
                        for j in range(nj):
                            g.wait_ge(s_ev, j + 1)
                            g.dma_start(out=out_dram[(j0 + j), :, ts(h, 512)],
                                        in_=sm["hq"][:, ts(j % 2, 512)]
                                        ).then_inc(s_st, 16)
                        g.wait_ge(s_st, 16 * nj)

    def p_proj_tm(self, tag, w_dram, dout, out_dram, transpose_w, rope):
        nc, sm = self.nc, self.sm
        nblk = dout // 512
        for dblk in range(nblk):
            self.free_phase_sems()
            tg = f"{tag}{dblk}"
            s_w = self.sem(tg + "w")
            s_mm = self.sem(tg + "m")
            s_cp = self.sem(tg + "c")
            s_rp = self.sem(tg + "r")
            s_st = self.sem(tg + "s")
            with nc.Block() as blk:
                @blk.sync
                def _(sy):
                    for k in range(16):
                        if k >= 2:
                            sy.wait_ge(s_mm, k - 1)
                        if transpose_w:
                            sy.dma_start(sm["rhld"][:, k % 2, 0:512],
                                         self.dr[w_dram][ts(dblk, 512), ts(k)],
                                         transpose=True).then_inc(s_w, 16)
                        else:
                            sy.dma_start(sm["rhld"][:, k % 2, 0:512],
                                         self.dr[w_dram][ts(k), ts(dblk, 512)]
                                         ).then_inc(s_w, 16)
                    sy.wait_ge(s_w, 256)

                @blk.tensor
                def _(te):
                    for k in range(16):
                        te.wait_ge(s_w, 16 * (k + 1))
                        for tt in range(8):
                            mm = te.matmul(self.ps[tt][:, :], self.xnb[:, k, ts(tt)],
                                           sm["rhld"][:, k % 2, 0:512],
                                           start=(k == 0), stop=(k == 15))
                        mm.then_inc(s_mm, 1)

                @blk.scalar
                def _(s):
                    s.wait_ge(s_mm, 16)
                    for tt in range(8):
                        if not rope:
                            if tt >= 2:
                                s.wait_ge(s_st, 16 * (tt - 1))
                            s.activation(sm["hq"][:, ts(tt % 2, 512)],
                                         self.ps[tt][:, :],
                                         AF.Identity).then_inc(s_cp, 1)
                        else:
                            if tt >= 2:
                                s.wait_ge(s_rp, tt - 1)
                            s.activation(sm["ev"][:, ts(tt % 2, 512)],
                                         self.ps[tt][:, :],
                                         AF.Identity).then_inc(s_cp, 1)

                if rope:
                    @blk.vector
                    def _(v):
                        for tt in range(8):
                            v.wait_ge(s_cp, tt + 1)
                            if tt >= 2:
                                v.wait_ge(s_st, 16 * (tt - 1))
                            src3 = sm["ev"][:, ts(tt % 2, 512)].rearrange(
                                "p (h d) -> p h d", d=128)
                            dst3 = sm["ev2"][:, ts(tt % 2, 512)].rearrange(
                                "p (h d) -> p h d", d=128)
                            x1, x2 = src3[:, :, 0:64], src3[:, :, 64:128]
                            o1, o2 = dst3[:, :, 0:64], dst3[:, :, 64:128]
                            cosb = sm["ropc"][:, tt, :].unsqueeze(1).broadcast_to(
                                [128, 4, 64])
                            sinb = sm["rops"][:, tt, :].unsqueeze(1).broadcast_to(
                                [128, 4, 64])
                            v.tensor_tensor(o1, x1, cosb, op=ALU.mult)
                            v.tensor_tensor(o2, x2, cosb, op=ALU.mult)
                            v.drain()
                            tmp = sm["bcast"][:, 0:512].rearrange(
                                "p (h d) -> p h d", d=128)
                            v.tensor_tensor(tmp[:, :, 0:64], x2, sinb, op=ALU.mult)
                            v.tensor_tensor(tmp[:, :, 64:128], x1, sinb, op=ALU.mult)
                            v.drain()
                            v.tensor_tensor(o1, o1, tmp[:, :, 0:64], op=ALU.subtract)
                            v.tensor_tensor(o2, o2, tmp[:, :, 64:128], op=ALU.add)
                            v.drain()
                            v.tensor_copy(sm["hq"][:, ts(tt % 2, 512)],
                                          sm["ev2"][:, ts(tt % 2, 512)])
                            v.drain()
                            v.sem_inc(s_rp, 1)

                @blk.gpsimd
                def _(g):
                    for tt in range(8):
                        g.wait_ge(s_rp if rope else s_cp, tt + 1)
                        g.dma_start(out=out_dram[ts(tt), ts(dblk, 512)],
                                    in_=sm["hq"][:, ts(tt % 2, 512)]
                                    ).then_inc(s_st, 16)
                    g.wait_ge(s_st, 128)

    def p_g0(self):
        nc, sm = self.nc, self.sm
        s_w = self.sem("g0w")
        s_mm = self.sem("g0m")
        s_ev = self.sem("g0e")
        s_st = self.sem("g0s")
        with nc.Block() as blk:
            @blk.sync
            def _(sy):
                for k in range(16):
                    sy.dma_start(sm["wld"][:, k // 8, k % 8, 0:8],
                                 self.dr["wg0_b"][ts(k), :]).then_inc(s_w, 16)
                sy.wait_ge(s_w, 256)

            @blk.tensor
            def _(te):
                te.wait_ge(s_w, 256)
                for k in range(16):
                    for h in range(2):
                        mm = te.matmul(self.ps[4 + h][0:8, :],
                                       sm["wld"][:, k // 8, k % 8, 0:8],
                                       self.xnb[:, k, ts(h, 512)],
                                       start=(k == 0), stop=(k == 15))
                        if k == 15:
                            mm.then_inc(s_mm, 1)

            @blk.vector
            def _(v):
                v.wait_ge(s_mm, 2)
                for h in range(2):
                    v.tensor_copy(sm["bcast"][0:8, ts(h, 512)],
                                  self.ps[4 + h][0:8, :])
                v.drain()
                v.sem_inc(s_ev, 1)

            @blk.gpsimd
            def _(g):
                g.wait_ge(s_ev, 1)
                g.dma_start(out=self.dr["g0_in"][:, :],
                            in_=sm["bcast"][0:8, 0:1024]).then_inc(s_st, 16)
                g.wait_ge(s_st, 16)

    def p_ag(self, tag, pairs):
        s_cc = self.sem(tag)
        self.cc_counts[tag] = len(pairs)
        for i, (src, dst) in enumerate(pairs):
            with self.nc.Block() as blk:
                @blk.gpsimd
                def _(g):
                    g.collective_compute(
                        "AllGather", ALU.bypass,
                        replica_groups=[[0, 1, 2, 3], [4, 5, 6, 7]],
                        ins=[self.dr[src].ap().opt()],
                        outs=[self.dr[dst].ap().opt()]).then_inc(s_cc, 1)
            with self.nc.Block() as blk:
                @blk.sync
                def _(sy):
                    sy.wait_ge(s_cc, i + 1)

    def p_logG(self):
        nc, sm = self.nc, self.sm
        s_l = self.sem("lgl")
        s_v = self.sem("lgv")
        s_t = self.sem("lgt")
        with nc.Block() as blk:
            @blk.sync
            def _(sy):
                sy.wait_ge(self.sem("cc_kv0"), self.cc_counts["cc_kv0"])
                for r in range(GRP):
                    sy.dma_start(self.g_sb[:, ts(r, 1024)],
                                 self.dr["g0_ag"][r, :, :]).then_inc(s_l, 16)
                sy.wait_ge(s_l, 64)

            @blk.vector
            def _(v):
                v.memset(sm["scal"][:, 14:15], -GATE_OFFSET)
                v.drain()

            @blk.scalar
            def _(s):
                s.wait_ge(s_l, 64)
                s.activation(self.u_sb, self.g_sb, AF.Exp,
                             bias=sm["scal"][0:8, 14:15], scale=-1.0)
                s.drain()
        with nc.Block() as blk:
            @blk.vector
            def _(v):
                v.memset(self.acc_sb, 0.0)
                v.drain()
                acc = self.acc_sb
                for k in range(10, 0, -1):
                    c = ((-1.0) ** (k + 1)) / k
                    v.scalar_tensor_tensor(acc, acc, c, self.u_sb,
                                           op0=ALU.add, op1=ALU.mult)
                    v.drain()
                v.tensor_scalar(acc, acc, -1.0, None, op0=ALU.mult)
                v.drain()
                v.tensor_tensor_scan(self.lg_sb, acc, acc, 0.0,
                                     op0=ALU.add, op1=ALU.bypass)
                v.drain()
        with nc.Block() as blk:
            @blk.scalar
            def _(s):
                s.activation(self.em_sb, self.lg_sb, AF.Exp, scale=-1.0)
                s.drain()
        with nc.Block() as blk:
            @blk.tensor
            def _(te):
                for st in range(NST):
                    if st >= 2:
                        te.wait_ge(s_v, st - 1)
                    te.transpose(self.ps[st % 2][0:128, 0:8],
                                 self.em_sb[:, ts(st)],
                                 sm["identf"][0:8, 0:8]).then_inc(s_t, 1)

            @blk.vector
            def _(v):
                for st in range(NST):
                    v.wait_ge(s_t, st + 1)
                    v.tensor_copy(self.emt[:, st, :], self.ps[st % 2][0:128, 0:8]
                                  ).then_inc(s_v, 1)
                    v.drain()
                for st in range(NST):
                    v.tensor_copy(self.emtb[:, st, :], self.emt[:, st, :])
                v.drain()
                v.sem_inc(s_l, 96)

            @blk.sync
            def _(sy):
                sy.wait_ge(s_l, 160)

    def p_ffn0_mm1(self):
        nc, sm = self.nc, self.sm
        s_pre = self.sem("f1pre")
        with nc.Block() as blk:
            @blk.sync
            def _(sy):
                sy.wait_ge(self.sem("cc_w"), 2)
                sy.dma_start(sm["sw1f"][:],
                             self.dr["sw1_ag"][:].rearrange("r o (i p) -> p (r i)",
                                                            p=128),
                             allow_slow_non_contiguous=True).then_inc(s_pre, 16)
                sy.wait_ge(s_pre, 16)

            @blk.vector
            def _(v):
                v.wait_ge(s_pre, 16)
                v.tensor_scalar(sm["sxw1"][:], sm["sw1f"][:],
                                sm["scal"][:, 0:1], None, op0=ALU.mult)
                v.drain()
        for grp in range(16):
            self.free_phase_sems()
            tg = f"f1g{grp}"
            s_w = self.sem(tg + "w")
            s_r = self.sem(tg + "r")
            s_mm = self.sem(tg + "m")
            s_ev = self.sem(tg + "e")
            s_rd = self.sem(tg + "d")
            s_st = self.sem(tg + "s")
            ics = [grp * 4 + j for j in range(4)]
            rb, off = ics[0] // 8, (ics[0] % 8) * 128
            with nc.Block() as blk:
                @blk.sync
                def _(sy):
                    for k in range(16):
                        if k >= 2:
                            sy.wait_ge(s_mm, k - 1)
                        sy.dma_start(sm["wld"][:, k % 2, 0:4, :],
                                     self.dr["qw1t_ag"][rb][ts(k), off:off + 512]
                                     .rearrange("p (j q) -> p j q", q=128)
                                     ).then_inc(s_w, 16)
                        sy.dma_start(sm["rhld"][:, k % 2, :],
                                     self.dr["qxn_d"][k, :, :]).then_inc(s_w, 16)
                    sy.wait_ge(s_w, 512)

                @blk.tensor
                def _(te):
                    for k in range(16):
                        te.wait_ge(s_w, 32 * (k + 1))
                        for j in range(4):
                            for h in range(2):
                                mm = te.matmul(self.ps[2 * j + h][:, :],
                                               sm["wld"][:, k % 2, j, :],
                                               sm["rhld"][:, k % 2, ts(h, 512)],
                                               start=(k == 0), stop=(k == 15))
                        mm.then_inc(s_mm, 1)

                @blk.scalar
                def _(s):
                    s.wait_ge(s_mm, 16)
                    for j in range(4):
                        ic = ics[j]
                        if j >= 2:
                            s.wait_ge(s_st, 16 * (j - 1))
                        for h in range(2):
                            dst = sm["ev"] if j % 2 == 0 else sm["ev2"]
                            s.activation(dst[:, ts(h, 512)], self.ps[2 * j + h][:, :],
                                         AF.Gelu, bias=sm["b1s"][:, ic:ic + 1],
                                         scale=sm["sxw1"][:, ic:ic + 1]
                                         ).then_inc(s_ev, 1)

                @blk.vector
                def _(v):
                    for j in range(4):
                        ic = ics[j]
                        v.wait_ge(s_ev, 2 * j + 2)
                        src_t = sm["ev"] if j % 2 == 0 else sm["ev2"]
                        v.tensor_reduce(sm["stat"][:, ic:ic + 1], src_t[:, 0:1024],
                                        axis=AX.X, op=ALU.max,
                                        apply_absolute_value=True)
                        v.drain()
                        v.sem_inc(s_rd, 1)

                @blk.gpsimd
                def _(g):
                    for j in range(4):
                        ic = ics[j]
                        g.wait_ge(s_rd, j + 1)
                        src_t = sm["ev"] if j % 2 == 0 else sm["ev2"]
                        g.dma_start(out=self.dr["h0_d"][ic, :, :],
                                    in_=src_t[:, 0:1024]).then_inc(s_st, 16)
                    g.wait_ge(s_st, 64)

    def p_hquant(self):
        nc, sm = self.nc, self.sm
        s_l = self.sem("hql")
        s_q = self.sem("hqq")
        s_s = self.sem("hqs")
        with nc.Block() as blk:
            @blk.sync
            def _(sy):
                for c in range(64):
                    if c >= 2:
                        sy.wait_ge(s_q, c - 1)
                    sy.dma_start(sm["ev"][:, 0:1024] if c % 2 == 0 else
                                 sm["ev2"][:, 0:1024],
                                 self.dr["h0_d"][c, :, :]).then_inc(s_l, 16)
                sy.wait_ge(s_l, 64 * 16)

            @blk.vector
            def _(v):
                for c in range(64):
                    v.wait_ge(s_l, 16 * (c + 1))
                    if c >= 2:
                        v.wait_ge(s_s, 16 * (c - 1))
                    v.tensor_scalar(sm["bcast"][:, 0:1024],
                                    sm["ev"][:, 0:1024] if c % 2 == 0 else
                                    sm["ev2"][:, 0:1024],
                                    sm["scal"][:, 24:25], C_RNE,
                                    op0=ALU.mult, op1=ALU.add)
                    v.drain()
                    v.tensor_scalar(sm["bcast"][:, 0:1024], sm["bcast"][:, 0:1024],
                                    C_RNE, 127.0, op0=ALU.subtract, op1=ALU.min)
                    v.drain()
                    v.tensor_scalar(sm["hq"][:] if c % 2 == 0 else
                                    sm["qslice"][:].rearrange("p a b -> p (a b)"),
                                    sm["bcast"][:, 0:1024], -127.0, None, op0=ALU.max
                                    ).then_inc(s_q, 1)
                    v.drain()

            @blk.gpsimd
            def _(g):
                for c in range(64):
                    g.wait_ge(s_q, c + 1)
                    g.dma_start(out=self.dr["hq_d"][c, :, :],
                                in_=sm["hq"][:, 0:1024] if c % 2 == 0 else
                                sm["qslice"][:].rearrange("p a b -> p (a b)")[:, 0:1024]
                                ).then_inc(s_s, 16)
                g.wait_ge(s_s, 64 * 16)

    def p_ffn0_mm2(self):
        nc, sm = self.nc, self.sm
        s_pre = self.sem("f2pre")
        with nc.Block() as blk:
            @blk.sync
            def _(sy):
                sy.wait_ge(self.sem("cc_w"), 4)
                sy.dma_start(sm["sw2f"][:],
                             self.dr["sw2_ag"][:].rearrange("r o (i p) -> p (r i)",
                                                            p=128),
                             allow_slow_non_contiguous=True).then_inc(s_pre, 16)
                sy.wait_ge(s_pre, 16)

            @blk.vector
            def _(v):
                v.wait_ge(s_pre, 16)
                v.tensor_scalar(sm["sxw2"][:], sm["sw2f"][:],
                                sm["scal"][:, 16:17], None, op0=ALU.mult)
                v.drain()
        for grp in range(4):
            self.free_phase_sems()
            tg = f"f2g{grp}"
            s_w = self.sem(tg + "w")
            s_mm = self.sem(tg + "m")
            s_ev = self.sem(tg + "e")
            s_ad = self.sem(tg + "a")
            dcs = [grp * 4 + j for j in range(4)]
            with nc.Block() as blk:
                @blk.sync
                def _(sy):
                    for k in range(64):
                        if k >= 2:
                            sy.wait_ge(s_mm, k - 1)
                        for jj in range(2):
                            dc0 = dcs[2 * jj]
                            rb = dc0 // 2
                            sy.dma_start(sm["wld"][:, k % 2, 2 * jj:2 * jj + 2, :],
                                         self.dr["qw2t_ag"][rb][ts(k), :]
                                         .rearrange("p (j q) -> p j q", q=128)
                                         ).then_inc(s_w, 16)
                        sy.dma_start(sm["rhld"][:, k % 2, :],
                                     self.dr["hq_d"][k, :, :]).then_inc(s_w, 16)
                    sy.wait_ge(s_w, 64 * 48)

                @blk.tensor
                def _(te):
                    for k in range(64):
                        te.wait_ge(s_w, 48 * (k + 1))
                        for j in range(4):
                            for h in range(2):
                                mm = te.matmul(self.ps[2 * j + h][:, :],
                                               sm["wld"][:, k % 2, j, :],
                                               sm["rhld"][:, k % 2, ts(h, 512)],
                                               start=(k == 0), stop=(k == 63))
                        mm.then_inc(s_mm, 1)

                @blk.scalar
                def _(s):
                    s.wait_ge(s_mm, 64)
                    for j in range(4):
                        dc = dcs[j]
                        for h in range(2):
                            if 2 * j + h >= 2:
                                s.wait_ge(s_ad, 2 * j + h - 1)
                            s.activation(sm["ev"][:, ts((2 * j + h) % 2, 512)],
                                         self.ps[2 * j + h][:, :], AF.Identity,
                                         bias=sm["b2s"][:, dc:dc + 1],
                                         scale=sm["sxw2"][:, dc:dc + 1]
                                         ).then_inc(s_ev, 1)

                @blk.vector
                def _(v):
                    for j in range(4):
                        dc = dcs[j]
                        for h in range(2):
                            v.wait_ge(s_ev, 2 * j + h + 1)
                            v.tensor_tensor(self.RES[:, dc, ts(h, 512)],
                                            self.RES[:, dc, ts(h, 512)],
                                            sm["ev"][:, ts((2 * j + h) % 2, 512)],
                                            op=ALU.add).then_inc(s_ad, 1)
                            v.drain()

    def p_fplus(self):
        nc, sm = self.nc, self.sm
        s_w = self.sem("fpw")
        s_mm = self.sem("fpm")
        s_ev = self.sem("fpe")
        s_st = self.sem("fps")
        with nc.Block() as blk:
            @blk.sync
            def _(sy):
                for st in range(NST):
                    if st >= 2:
                        sy.wait_ge(s_mm, 2 * (st - 1))
                    sy.dma_start(sm["rhld"][:, st % 2, :],
                                 self.inp["selhot"][st, :, :]).then_inc(s_w, 16)
                sy.wait_ge(s_w, 16 * NST)

            @blk.tensor
            def _(te):
                for st in range(NST):
                    te.wait_ge(s_w, 16 * (st + 1))
                    for h in range(2):
                        te.matmul(self.ps[6 + h][0:8, :], self.emtb[:, st, :],
                                  sm["rhld"][:, st % 2, ts(h, 512)],
                                  start=(st == 0), stop=(st == NST - 1)
                                  ).then_inc(s_mm, 1)

            @blk.vector
            def _(v):
                v.wait_ge(s_mm, 2 * NST)
                for h in range(2):
                    v.tensor_copy(sm["bcast"][0:8, ts(h, 512)],
                                  self.ps[6 + h][0:8, :])
                v.drain()
                v.sem_inc(s_ev, 1)

            @blk.gpsimd
            def _(g):
                g.wait_ge(s_ev, 1)
                g.dma_start(out=self.dr["fpl_d"][:, :],
                            in_=sm["bcast"][0:8, 0:1024]).then_inc(s_st, 16)
                g.wait_ge(s_st, 16)

    def p_attn(self, layer):
        nc, sm = self.nc, self.sm
        L0 = (layer == 0)
        pairs = [(kv, 0) for kv in range(8)] if L0 else \
            [(kv, pr) for kv in range(4) for pr in range(2)]
        for qc in range(2):
            for pi, (kv, pr) in enumerate(pairs):
                self.free_phase_sems()
                tg = f"a{layer}_{qc}_{pi}"
                s_q = self.sem(tg + "q")
                s_k = self.sem(tg + "k")
                s_m = self.sem(tg + "m")
                s_vt = self.sem(tg + "v")
                s_sc = self.sem(tg + "s")
                s_sq = self.sem(tg + "x")
                s_msk = self.sem(tg + "y")
                s_y = self.sem(tg + "z")
                s_f = self.sem(tg + "f")
                s_dn = self.sem(tg + "d")
                s_bc = self.sem(tg + "b")
                s_ev = self.sem(tg + "e")
                s_st = self.sem(tg + "t")
                heads = [kv * 2 + h2 for h2 in range(2)] if L0 else \
                    [kv * 4 + pr * 2 + h2 for h2 in range(2)]
                with nc.Block() as blk:
                    @blk.sync
                    def _(sy):
                        for h2 in range(2):
                            hh = heads[h2]
                            if L0:
                                sy.dma_start(sm["qslice"][:, h2, :],
                                             self.dr["q0_d"][hh, :, ts(qc, 512)]
                                             ).then_inc(s_q, 16)
                            else:
                                sy.dma_start(sm["qslice"][:, h2, :],
                                             self.dr["q1_d"][ts(qc, 512), ts(hh)],
                                             transpose=True).then_inc(s_q, 16)
                        if L0:
                            sy.dma_start(sm["row"][0:1, 0:512],
                                         self.dr["fpl_d"][kv, ts(qc, 512)]
                                         ).then_inc(s_f, 16)
                        for st in range(NST):
                            rb, j = st // 8, st % 8
                            if st >= 2:
                                sy.wait_ge(s_sc, 2 * (st - 1))
                            if L0:
                                sy.dma_start(sm["ktile"][:, st % 2, :],
                                             self.dr["k0_ag"][rb, kv, :, ts(j)]
                                             ).then_inc(s_k, 16)
                            else:
                                sy.dma_start(sm["ktile"][:, st % 2, :],
                                             self.dr["k1_ag"][rb][ts(j), ts(kv)],
                                             transpose=True).then_inc(s_k, 16)
                            if st >= 2:
                                sy.wait_ge(s_y, 2 * (st - 1))
                            sy.dma_start(sm["vtile"][:, st % 2, :],
                                         self.dr["v0_ag" if L0 else "v1_ag"]
                                         [rb][ts(j), ts(kv)]).then_inc(s_k, 16)

                    @blk.gpsimd
                    def _(g):
                        for st in range(NST):
                            if st >= 2:
                                g.wait_ge(s_msk, 2 * (st - 1))
                            g.dma_start(out=sm["mtile"][:, st % 2, :],
                                        in_=self.inp["mask01"][st, :, ts(qc, 512)]
                                        ).then_inc(s_m, 16)
                        g.wait_ge(s_dn, 2)
                        g.dma_start(out=self.dr["flat1"][0:1, 2560:3584],
                                    in_=sm["row2"][0:1, 0:1024]
                                    ).then_inc(s_bc, 16)
                        g.wait_ge(s_bc, 16)
                        g.dma_start(out=sm["bcast"][:, 0:1024],
                                    in_=self.dr["flat1"][0:1, 2560:3584]
                                    .broadcast_to([128, 1024])).then_inc(s_bc, 16)
                        g.wait_ge(s_bc, 32)

                    @blk.vector
                    def _(v):
                        for st in range(NST):
                            if L0:
                                v.wait_ge(s_k, 32 * (st + 1))
                                v.tensor_scalar(sm["vtile"][:, st % 2, :],
                                                sm["vtile"][:, st % 2, :],
                                                self.emt[:, st, kv:kv + 1], None,
                                                op0=ALU.mult).then_inc(s_vt, 1)
                                v.drain()
                            for h2 in range(2):
                                v.wait_ge(s_sq, 2 * st + h2 + 1)
                                v.wait_ge(s_m, 16 * (st + 1))
                                v.tensor_tensor(sm["atile"][:, h2, :],
                                                sm["atile"][:, h2, :],
                                                sm["mtile"][:, st % 2, :],
                                                op=ALU.mult).then_inc(s_msk, 1)
                                v.drain()
                        for h2 in range(2):
                            v.wait_ge(s_y, 2 * NST)
                            if L0:
                                v.wait_ge(s_f, 16)
                                v.scalar_tensor_tensor(
                                    sm["row2"][0:1, ts(h2, 512)],
                                    sm["row"][0:1, 0:512], 1e-6,
                                    self.ps[4 + h2][0:1, :],
                                    op0=ALU.mult, op1=ALU.add)
                            else:
                                v.tensor_scalar(sm["row2"][0:1, ts(h2, 512)],
                                                self.ps[4 + h2][0:1, :],
                                                1.0, None, op0=ALU.mult)
                            v.drain()
                            v.reciprocal(sm["row2"][0:1, ts(h2, 512)],
                                         sm["row2"][0:1, ts(h2, 512)])
                            v.drain()
                            v.sem_inc(s_dn, 1)
                        v.wait_ge(s_bc, 32)
                        for h2 in range(2):
                            v.tensor_tensor(sm["hq"][:, ts(h2, 512)],
                                            self.ps[2 + h2][:, :],
                                            sm["bcast"][:, ts(h2, 512)],
                                            op=ALU.mult).then_inc(s_ev, 1)
                            v.drain()

                    @blk.scalar
                    def _(s):
                        for st in range(NST):
                            for h2 in range(2):
                                s.wait_ge(s_sc, 2 * st + h2 + 1)
                                if st >= 1:
                                    s.wait_ge(s_y, 2 * (st - 1) + h2 + 1)
                                if L0:
                                    s.activation(sm["atile"][:, h2, :],
                                                 self.ps[h2][:, :], AF.Square,
                                                 scale=SCALE_QK).then_inc(s_sq, 1)
                                else:
                                    hh = heads[h2]
                                    s.activation(sm["atile"][:, h2, :],
                                                 self.ps[h2][:, :], AF.Exp,
                                                 scale=SCALE_QK,
                                                 bias=sm["mhat"][:, hh:hh + 1]
                                                 ).then_inc(s_sq, 1)

                    @blk.tensor
                    def _(te):
                        te.wait_ge(s_q, 32)
                        for st in range(NST):
                            te.wait_ge(s_k, 32 * (st + 1))
                            if L0:
                                te.wait_ge(s_vt, st + 1)
                            for h2 in range(2):
                                if st >= 1:
                                    te.wait_ge(s_sq, 2 * (st - 1) + h2 + 1)
                                te.matmul(self.ps[h2][:, :],
                                          sm["ktile"][:, st % 2, :],
                                          sm["qslice"][:, h2, :],
                                          start=True, stop=True).then_inc(s_sc, 1)
                            for h2 in range(2):
                                te.wait_ge(s_msk, 2 * st + h2 + 1)
                                aap = sm["atile"][:, h2, :]
                                te.matmul(self.ps[2 + h2][:, :],
                                          sm["vtile"][:, st % 2, :], aap,
                                          start=(st == 0), stop=(st == NST - 1))
                                dl = (self.emtb[:, st, kv:kv + 1] if L0
                                      else sm["onesb"][:, 0:1])
                                te.matmul(self.ps[4 + h2][0:1, :], dl, aap,
                                          start=(st == 0), stop=(st == NST - 1)
                                          ).then_inc(s_y, 1)

                with nc.Block() as blk:
                    @blk.sync
                    def _(sy):
                        for h2 in range(2):
                            hh = heads[h2]
                            sy.dma_start(self.dr["y_d"][hh, :, ts(qc, 512)],
                                         sm["hq"][:, ts(h2, 512)]
                                         ).then_inc(s_st, 16)
                        sy.wait_ge(s_st, 32)

    def p_outproj(self, tag, w_dram, bias_t):
        nc, sm = self.nc, self.sm
        for grp in range(4):
            self.free_phase_sems()
            tg = f"{tag}{grp}"
            s_w = self.sem(tg + "w")
            s_mm = self.sem(tg + "m")
            s_ev = self.sem(tg + "e")
            s_ad = self.sem(tg + "a")
            dcs = [grp * 4 + j for j in range(4)]
            with nc.Block() as blk:
                @blk.sync
                def _(sy):
                    for k in range(16):
                        if k >= 2:
                            sy.wait_ge(s_mm, k - 1)
                        sy.dma_start(sm["wld"][:, k % 2, 0:4, :],
                                     self.dr[w_dram][ts(k),
                                                     dcs[0] * 128:dcs[0] * 128 + 512]
                                     .rearrange("p (j q) -> p j q", q=128)
                                     ).then_inc(s_w, 16)
                        sy.dma_start(sm["rhld"][:, k % 2, :],
                                     self.dr["y_d"][k, :, :]).then_inc(s_w, 16)
                    sy.wait_ge(s_w, 512)

                @blk.tensor
                def _(te):
                    for k in range(16):
                        te.wait_ge(s_w, 32 * (k + 1))
                        for j in range(4):
                            for h in range(2):
                                mm = te.matmul(self.ps[2 * j + h][:, :],
                                               sm["wld"][:, k % 2, j, :],
                                               sm["rhld"][:, k % 2, ts(h, 512)],
                                               start=(k == 0), stop=(k == 15))
                        mm.then_inc(s_mm, 1)

                @blk.scalar
                def _(s):
                    s.wait_ge(s_mm, 16)
                    for j in range(4):
                        dc = dcs[j]
                        for h in range(2):
                            if 2 * j + h >= 2:
                                s.wait_ge(s_ad, 2 * j + h - 1)
                            if bias_t is not None:
                                s.activation(sm["ev"][:, ts((2 * j + h) % 2, 512)],
                                             self.ps[2 * j + h][:, :], AF.Identity,
                                             bias=sm[bias_t][:, dc:dc + 1]
                                             ).then_inc(s_ev, 1)
                            else:
                                s.activation(sm["ev"][:, ts((2 * j + h) % 2, 512)],
                                             self.ps[2 * j + h][:, :], AF.Identity
                                             ).then_inc(s_ev, 1)

                @blk.vector
                def _(v):
                    for j in range(4):
                        dc = dcs[j]
                        for h in range(2):
                            v.wait_ge(s_ev, 2 * j + h + 1)
                            v.tensor_tensor(self.RES[:, dc, ts(h, 512)],
                                            self.RES[:, dc, ts(h, 512)],
                                            sm["ev"][:, ts((2 * j + h) % 2, 512)],
                                            op=ALU.add).then_inc(s_ad, 1)
                            v.drain()

    def p_xnb_store(self):
        nc = self.nc
        s = self.sem("xnbs")
        with nc.Block() as blk:
            @blk.sync
            def _(sy):
                for c in range(16):
                    sy.dma_start(self.dr["xn1b_d"][c, :, :],
                                 self.xnb[:, c, :]).then_inc(s, 16)
                sy.wait_ge(s, 256)

    def p_mhat(self):
        nc, sm = self.nc, self.sm
        s_l = self.sem("mhl")
        s_v = self.sem("mhv")
        s_d = self.sem("mhd")
        rh = sm["rhld"][:].rearrange("p a b -> p (a b)")
        with nc.Block() as blk:
            @blk.vector
            def _(v):
                v.memset(sm["stat"][:, 0:24], 0.0)
                v.drain()
        for tt in range(8):
            with nc.Block() as blk:
                @blk.sync
                def _(sy):
                    sy.dma_start(rh, self.dr["q1_d"][ts(tt), :]).then_inc(s_l, 16)
                    sy.wait_ge(s_l, 16 * (tt + 1))

                @blk.scalar
                def _(s):
                    s.wait_ge(s_l, 16 * (tt + 1))
                    s.activation(sm["ev"][:, 0:1024], rh[:, 0:1024], AF.Square)
                    s.activation(sm["ev2"][:, 0:1024], rh[:, 1024:2048], AF.Square)
                    s.drain()

            with nc.Block() as blk:
                @blk.vector
                def _(v):
                    v.tensor_reduce(sm["wps"][:, 0:8],
                                    sm["ev"][:, 0:1024].rearrange(
                                        "p (h d) -> p h d", d=128),
                                    axis=AX.X, op=ALU.add)
                    v.drain()
                    v.tensor_reduce(sm["wpr"][:, 0:8],
                                    sm["ev2"][:, 0:1024].rearrange(
                                        "p (h d) -> p h d", d=128),
                                    axis=AX.X, op=ALU.add)
                    v.drain()
                    v.tensor_tensor(sm["stat"][:, 0:8], sm["stat"][:, 0:8],
                                    sm["wps"][:, 0:8], op=ALU.max)
                    v.tensor_tensor(sm["stat"][:, 8:16], sm["stat"][:, 8:16],
                                    sm["wpr"][:, 0:8], op=ALU.max)
                    v.drain()
        for rb in range(GRP):
            for j in range(8):
                with nc.Block() as blk:
                    @blk.sync
                    def _(sy):
                        sy.dma_start(rh[:, 0:512],
                                     self.dr["k1_ag"][rb][ts(j), :]
                                     ).then_inc(s_l, 16)
                        sy.wait_ge(s_l, 128 + 16 * (rb * 8 + j + 1))

                    @blk.scalar
                    def _(s):
                        s.wait_ge(s_l, 128 + 16 * (rb * 8 + j + 1))
                        s.activation(sm["ev"][:, 0:512], rh[:, 0:512], AF.Square)
                        s.drain()

                with nc.Block() as blk:
                    @blk.vector
                    def _(v):
                        v.tensor_reduce(sm["wps"][:, 0:4],
                                        sm["ev"][:, 0:512].rearrange(
                                            "p (h d) -> p h d", d=128),
                                        axis=AX.X, op=ALU.add)
                        v.drain()
                        v.tensor_tensor(sm["stat"][:, 16:20], sm["stat"][:, 16:20],
                                        sm["wps"][:, 0:4], op=ALU.max)
                        v.drain()
        with nc.Block() as blk:
            @blk.tensor
            def _(te):
                te.transpose(self.ps[6][0:20, 0:128], sm["stat"][:, 0:20],
                             sm["identf"][:]).then_inc(s_d, 1)
        with nc.Block() as blk:
            @blk.vector
            def _(v):
                v.tensor_reduce(sm["bcast"][0:20, 0:1], self.ps[6][0:20, 0:128],
                                axis=AX.X, op=ALU.max)
                v.drain()
                v.sem_inc(s_d, 1)

            @blk.tensor
            def _(te):
                te.wait_ge(s_d, 2)
                te.transpose(self.ps[7][0:1, 0:20], sm["bcast"][0:20, 0:1],
                             sm["identf"][0:20, 0:20]).then_inc(s_d, 1)
        with nc.Block() as blk:
            @blk.vector
            def _(v):
                v.tensor_copy(sm["row"][0:1, 0:20], self.ps[7][0:1, 0:20])
                v.drain()
                v.sem_inc(s_d, 61)
        with nc.Block() as blk:
            @blk.vector
            def _(v):
                v.tensor_tensor(sm["row"][0:1, 32:48].rearrange(
                                    "o (a b) -> o a b", b=4),
                                sm["row"][0:1, 0:16].rearrange(
                                    "o (a b) -> o a b", b=4),
                                sm["row"][0:1, 16:20].unsqueeze(2)
                                .broadcast_to([1, 4, 4]), op=ALU.mult)
                v.drain()

            @blk.scalar
            def _(s):
                s.drain()
        with nc.Block() as blk:
            @blk.scalar
            def _(s):
                s.activation(sm["row"][0:1, 64:80], sm["row"][0:1, 32:48], AF.Sqrt)
                s.drain()
        with nc.Block() as blk:
            @blk.vector
            def _(v):
                v.tensor_scalar(sm["row"][0:1, 96:112], sm["row"][0:1, 64:80],
                                -SCALE_QK, None, op0=ALU.mult)
                v.drain()
        with nc.Block() as blk:
            @blk.sync
            def _(sy):
                sy.dma_start(self.dr["flat1"][0:1, 2100:2116],
                             sm["row"][0:1, 96:112]).then_inc(s_d, 16)
                sy.wait_ge(s_d, 80)
                sy.dma_start(sm["mhat"][:, 0:16],
                             self.dr["flat1"][0:1, 2100:2116]
                             .broadcast_to([128, 16])).then_inc(s_d, 16)
                sy.wait_ge(s_d, 96)

    def p_dqx(self):
        nc, sm = self.nc, self.sm
        qsl = sm["qslice"][:].rearrange("p a b -> p (a b)")
        s_p = self.sem("dqp")
        with nc.Block() as blk:
            @blk.sync
            def _(sy):
                sy.dma_start(sm["ev"][0:8, 0:128],
                             self.dr["flat1"][0:1, 0:1024]
                             .rearrange("o (p f) -> (o p) f", p=8)
                             ).then_inc(s_p, 16)
                sy.dma_start(sm["ev"][0:8, 128:256],
                             self.dr["flat1"][0:1, 1024:2048]
                             .rearrange("o (p f) -> (o p) f", p=8)
                             ).then_inc(s_p, 16)

            @blk.tensor
            def _(te):
                te.wait_ge(s_p, 32)
                te.transpose(self.ps[6][:, 0:8], sm["ev"][0:8, 0:128],
                             sm["identf"][0:8, 0:8])
                te.transpose(self.ps[7][:, 0:8], sm["ev"][0:8, 128:256],
                             sm["identf"][0:8, 0:8]).then_inc(s_p, 1)

            @blk.vector
            def _(v):
                v.wait_ge(s_p, 33)
                v.tensor_copy(sm["mhat"][:, 0:8], self.ps[6][:, 0:8])
                v.drain()
                v.tensor_copy(sm["mhat"][:, 8:16], self.ps[7][:, 0:8])
                v.drain()
        self.free_phase_sems()
        for tt in range(8):
            for hh in range(2):
                s_t = self.sem(f"dqt{tt}_{hh}")
                s_q = self.sem(f"dqq{tt}_{hh}")
                s_s = self.sem(f"dqs{tt}_{hh}")
                with nc.Block() as blk:
                    @blk.tensor
                    def _(te):
                        for i in range(8):
                            c = hh * 8 + i
                            te.transpose(
                                self.ps[i][:, 0:128],
                                self.RES[:, c, ts(tt)],
                                sm["identf"][:]).then_inc(s_t, 1)

                    @blk.vector
                    def _(v):
                        for i in range(8):
                            v.wait_ge(s_t, i + 1)
                            v.tensor_scalar(
                                sm["ev"][:, i * 128:i * 128 + 128],
                                self.ps[i][:, 0:128],
                                sm["mhat"][:, tt:tt + 1], None, op0=ALU.mult)
                            v.drain()
                        v.tensor_scalar(sm["ev"][:, 0:1024], sm["ev"][:, 0:1024],
                                        sm["mhat"][:, 8 + tt:9 + tt], None,
                                        op0=ALU.add)
                        v.drain()
                        self.v_mx_dq(v, sm["ev"][:, 0:1024], qsl, 1024,
                                     sm["ev2"][:, 0:1024], sm["bcast"][:, 0:1024],
                                     sm["bcast2"][:, 0:512].bitcast(U16),
                                     sm["bcast2"][:, 512:544],
                                     sm["bcast2"][:, 544:576],
                                     sm["bcast2"][:, 576:608])
                        v.sem_inc(s_q, 1)

                    @blk.sync
                    def _(sy):
                        sy.wait_ge(s_q, 1)
                        sy.dma_start(
                            self.dr["dqx1_d"][ts(tt), hh * 1024:hh * 1024 + 1024],
                            qsl).then_inc(s_s, 16)
                        sy.wait_ge(s_s, 16)
                self.free_phase_sems()
        s_r = self.sem("dqr")
        with nc.Block() as blk:
            @blk.sync
            def _(sy):
                for c in range(16):
                    sy.dma_start(self.xnb[:, c, :],
                                 self.dr["dqx1_d"][:, ts(c)],
                                 transpose=True).then_inc(s_r, 16)
                sy.wait_ge(s_r, 256)

    def p_gup(self):
        nc, sm = self.nc, self.sm
        for tth in range(2):
            for ib in range(16):
                self.free_phase_sems()
                tg = f"gu{tth}_{ib}"
                s_w = self.sem(tg + "w")
                s_mm = self.sem(tg + "m")
                s_sl = self.sem(tg + "l")
                s_h2 = self.sem(tg + "h")
                s_st = self.sem(tg + "s")
                ig = ib * 512
                rb, off = ig // 1024, ig % 1024
                with nc.Block() as blk:
                    @blk.sync
                    def _(sy):
                        for k in range(16):
                            if k >= 2:
                                sy.wait_ge(s_mm, k - 1)
                            sy.dma_start(sm["rhld"][:, k % 2, 0:512],
                                         self.dr["dqgt_ag"][rb][ts(k),
                                                               off:off + 512]
                                         ).then_inc(s_w, 16)
                            sy.dma_start(sm["qslice"][:, k % 2, :],
                                         self.dr["dqut_ag"][rb][ts(k),
                                                                off:off + 512]
                                         ).then_inc(s_w, 16)
                        sy.wait_ge(s_w, 512)

                    @blk.tensor
                    def _(te):
                        for k in range(16):
                            te.wait_ge(s_w, 32 * (k + 1))
                            for t4 in range(4):
                                tt = tth * 4 + t4
                                te.matmul(self.ps[t4][:, :],
                                          self.xnb[:, k, ts(tt)],
                                          sm["rhld"][:, k % 2, 0:512],
                                          start=(k == 0), stop=(k == 15))
                                mm = te.matmul(self.ps[4 + t4][:, :],
                                               self.xnb[:, k, ts(tt)],
                                               sm["qslice"][:, k % 2, :],
                                               start=(k == 0), stop=(k == 15))
                            mm.then_inc(s_mm, 1)

                    @blk.scalar
                    def _(s):
                        s.wait_ge(s_mm, 16)
                        for t4 in range(4):
                            if t4 >= 2:
                                s.wait_ge(s_h2, t4 - 1)
                            s.activation(sm["ev"][:, ts(t4 % 2, 512)],
                                         self.ps[t4][:, :],
                                         AF.Silu).then_inc(s_sl, 1)

                    @blk.vector
                    def _(v):
                        for t4 in range(4):
                            v.wait_ge(s_sl, t4 + 1)
                            if t4 >= 2:
                                v.wait_ge(s_st, 16 * (t4 - 1))
                            v.tensor_tensor(sm["ev2"][:, 0:512],
                                            sm["ev"][:, ts(t4 % 2, 512)],
                                            self.ps[4 + t4][:, :], op=ALU.mult)
                            v.drain()
                            self.v_mx_dq(v, sm["ev2"][:, 0:512],
                                         sm["hq"][:, ts(t4 % 2, 512)], 512,
                                         sm["bcast"][:, 0:512],
                                         sm["bcast"][:, 512:1024],
                                         sm["bcast2"][:, 0:256].bitcast(U16),
                                         sm["stat"][:, 0:16], sm["stat"][:, 16:32],
                                         sm["stat"][:, 32:48])
                            v.sem_inc(s_h2, 1)

                    @blk.gpsimd
                    def _(g):
                        for t4 in range(4):
                            tt = tth * 4 + t4
                            g.wait_ge(s_h2, t4 + 1)
                            g.dma_start(out=self.dr["h2_d"][ts(tt), ig:ig + 512],
                                        in_=sm["hq"][:, ts(t4 % 2, 512)]
                                        ).then_inc(s_st, 16)
                        g.wait_ge(s_st, 64)

    def p_down(self):
        nc, sm = self.nc, self.sm
        for grp in range(4):
            self.free_phase_sems()
            tg = f"dn{grp}"
            s_w = self.sem(tg + "w")
            s_mm = self.sem(tg + "m")
            s_ev = self.sem(tg + "e")
            s_ad = self.sem(tg + "a")
            dcs = [grp * 4 + j for j in range(4)]
            with nc.Block() as blk:
                @blk.sync
                def _(sy):
                    for k in range(64):
                        if k >= 2:
                            sy.wait_ge(s_mm, k - 1)
                        for jj in range(2):
                            dc0 = dcs[2 * jj]
                            rb = dc0 // 2
                            sy.dma_start(sm["wld"][:, k % 2, 2 * jj:2 * jj + 2, :],
                                         self.dr["dqdt_ag"][rb][ts(k), :]
                                         .rearrange("p (j q) -> p j q", q=128)
                                         ).then_inc(s_w, 16)
                        sy.dma_start(sm["rhld"][:, k % 2, :],
                                     self.dr["h2_d"][:, ts(k)],
                                     transpose=True).then_inc(s_w, 16)
                    sy.wait_ge(s_w, 64 * 48)

                @blk.tensor
                def _(te):
                    for k in range(64):
                        te.wait_ge(s_w, 48 * (k + 1))
                        for j in range(4):
                            for h in range(2):
                                mm = te.matmul(self.ps[2 * j + h][:, :],
                                               sm["wld"][:, k % 2, j, :],
                                               sm["rhld"][:, k % 2, ts(h, 512)],
                                               start=(k == 0), stop=(k == 63))
                        mm.then_inc(s_mm, 1)

                @blk.scalar
                def _(s):
                    s.wait_ge(s_mm, 64)
                    for j in range(4):
                        for h in range(2):
                            if 2 * j + h >= 2:
                                s.wait_ge(s_ad, 2 * j + h - 1)
                            s.activation(sm["ev"][:, ts((2 * j + h) % 2, 512)],
                                         self.ps[2 * j + h][:, :], AF.Identity
                                         ).then_inc(s_ev, 1)

                @blk.vector
                def _(v):
                    for j in range(4):
                        dc = dcs[j]
                        for h in range(2):
                            v.wait_ge(s_ev, 2 * j + h + 1)
                            v.tensor_tensor(self.RES[:, dc, ts(h, 512)],
                                            self.RES[:, dc, ts(h, 512)],
                                            sm["ev"][:, ts((2 * j + h) % 2, 512)],
                                            op=ALU.add).then_inc(s_ad, 1)
                            v.drain()

    def p_store(self):
        nc = self.nc
        s = self.sem("fin")
        with nc.Block() as blk:
            @blk.sync
            def _(sy):
                for nm, cnt in self.cc_counts.items():
                    sy.wait_ge(self.sem(nm), cnt)
                for c in range(16):
                    sy.dma_start(self.out_fm[c, :, :],
                                 self.RES[:, c, :]).then_inc(s, 16)
                sy.wait_ge(s, 256)

    def p1(self):
        self._prep_shard("w1_shard", 1024, D, "qw1t_in", "sw1_in")
        self._prep_shard("w2_shard", 256, I_FFN, "qw2t_in", "sw2_in")
        self._prep_shard("gate_shard", 1024, D, "dqgt_in")
        self._prep_shard("up_shard", 1024, D, "dqut_in")
        self._prep_shard("down_shard", 256, I_FFN, "dqdt_in")
        s_cc = self.sem("cc_w")
        self.cc_counts = {"cc_w": 7}
        for i, (src, dst) in enumerate([
                ("qw1t_in", "qw1t_ag"), ("sw1_in", "sw1_ag"),
                ("qw2t_in", "qw2t_ag"), ("sw2_in", "sw2_ag"),
                ("dqgt_in", "dqgt_ag"), ("dqut_in", "dqut_ag"),
                ("dqdt_in", "dqdt_ag")]):
            with self.nc.Block() as blk:
                @blk.gpsimd
                def _(g):
                    g.collective_compute(
                        "AllGather", ALU.bypass,
                        replica_groups=[list(range(NCORE))],
                        ins=[self.dr[src].ap().opt()],
                        outs=[self.dr[dst].ap().opt()],
                    ).then_inc(s_cc, 1)
            with self.nc.Block() as blk:
                @blk.sync
                def _(sy):
                    sy.wait_ge(s_cc, i + 1)

    def debug_dump(self, names):
        nc = self.nc
        s_f = self.sem("dbgf")
        with nc.Block() as blk:
            @blk.sync
            def _(sy):
                for snm, cnt in getattr(self, "cc_counts", {}).items():
                    sy.wait_ge(self.sem(snm), cnt)
                n = 0
                for nm in names:
                    src = self.dr[nm]
                    o = nc.dram_tensor("dbg_" + nm, list(src.shape), src.dtype,
                                       kind="ExternalOutput")
                    self.dbg[nm] = o
                    sy.dma_start(o[:], src[:]).then_inc(s_f, 16)
                    n += 1
                sy.wait_ge(s_f, 16 * n)

    def finish_dummy(self):
        nc = self.nc
        s_f = self.sem("fin")
        with nc.Block() as blk:
            @blk.vector
            def _(v):
                v.memset(self.sm["ev"][:, 0:1024], 0.0)
                v.drain()

            @blk.sync
            def _(sy):
                for c in range(16):
                    sy.dma_start(self.out_fm[c, :, :],
                                 self.sm["ev"][:, 0:1024]).then_inc(s_f, 16)
                sy.wait_ge(s_f, 16 * 16)

    def build(self, debug_names=()):
        F = self.free_phase_sems
        self.p0(); F()
        self.p1(); F()
        if SERIALIZE_CC:
            with self.nc.Block() as blk:
                @blk.sync
                def _(sy):
                    sy.wait_ge(self.sem("cc_w"), 7)
            F()
        if STAGE >= 2:
            self.p_ln("ln0", load_x=True); F()
            self.p_scale("sc1", 16, "mx1_in", "mx1_out", 0, 8, 127.0); F()
            self.p_qxn("qx0", 8); F()
        if STAGE >= 3:
            self.p_proj_fm("q0p", "wq0_b", 2048, self.dr["q0_d"]); F()
            self.p_proj_fm("k0p", "wk0_b", 1024, self.dr["k0_in"]); F()
            self.p_proj_tm("v0p", "wv0_b", 1024, self.dr["v0_in"], False, False); F()
            self.p_g0(); F()
            self.p_ag("cc_kv0", [("k0_in", "k0_ag"), ("v0_in", "v0_ag"),
                                 ("g0_in", "g0_ag")])
        if STAGE >= 4:
            self.p_ffn0_mm1(); F()
            self.p_scale("sc2", 64, "mx2_in", "mx2_out", 16, 24, 127.0); F()
            self.p_hquant(); F()
            self.p_ffn0_mm2(); F()
        if STAGE >= 5:
            self.p_logG(); F()
            self.p_fplus(); F()
            self.p_attn(0); F()
            self.p_outproj("oc0", "wc0_b", "bcs"); F()
            if "x1_d" in debug_names:
                s_x1 = self.sem("x1dmp")
                with self.nc.Block() as blk:
                    @blk.sync
                    def _(sy):
                        for c in range(16):
                            sy.dma_start(self.dr["x1_d"][c, :, :],
                                         self.RES[:, c, :]).then_inc(s_x1, 16)
                        sy.wait_ge(s_x1, 256)
                F()
        if STAGE >= 6:
            if SUB6 >= 1:
                self.p_ln("ln1", load_x=False); F()
                if "xn1b_d" in debug_names:
                    self.p_xnb_store(); F()
            if SUB6 >= 2:
                self.p_proj_tm("q1p", "wq1_b", 2048, self.dr["q1_d"], True, True); F()
            if SUB6 >= 3:
                self.p_proj_tm("k1p", "wk1_b", 512, self.dr["k1_in"], True, True); F()
                self.p_proj_tm("v1p", "wv1_b", 512, self.dr["v1_in"], True, False); F()
            if SUB6 >= 4:
                self.p_ag("cc_kv1", [("k1_in", "k1_ag"), ("v1_in", "v1_ag")])
            if SUB6 >= 5:
                self.p_dqx(); F()
        if STAGE >= 7:
            self.p_mhat(); F()
            self.p_attn(1); F()
            self.p_outproj("op1", "wp1_b", None); F()
        if STAGE >= 8:
            self.p_gup(); F()
            self.p_down(); F()
        if debug_names:
            self.debug_dump(debug_names)
        if STAGE >= 9:
            self.p_store()
        else:
            self.finish_dummy()
        return self.nc


ROPE_BASE = 10000.0


def _stage_inputs(inputs):
    x = np.ascontiguousarray(inputs["L0_attn_hidden_states"])
    maps = []
    pos = np.arange(T, dtype=np.float32)
    half = 64
    inv = (1.0 / (ROPE_BASE ** (np.arange(half, dtype=np.float32) / half))).astype(np.float32)
    ang = pos[:, None] * inv[None, :]
    cos_t = np.cos(ang).astype(np.float32)
    sin_t = np.sin(ang).astype(np.float32)
    sg = np.arange(T)
    for c in range(NCORE):
        b, r = c // GRP, (c % GRP) * TOK
        m = {}
        xs = x[b, r:r + TOK, :]
        m["x_fm"] = np.ascontiguousarray(xs.T).reshape(16, 128, TOK)
        s_idx = sg.reshape(NST, 128)
        q_idx = r + np.arange(TOK)
        m["mask01"] = (s_idx[:, :, None] <= q_idx[None, None, :]).astype(ml_dtypes.bfloat16)
        m["selhot"] = (s_idx[:, :, None] == q_idx[None, None, :]).astype(ml_dtypes.bfloat16)
        tloc = r + np.arange(8)[None, :] * 128 + np.arange(128)[:, None]
        m["rope_cos"] = np.ascontiguousarray(cos_t[tloc])
        m["rope_sin"] = np.ascontiguousarray(sin_t[tloc])
        m["b1_r"] = np.ascontiguousarray(inputs["L0_ffn_b1"].reshape(64, 128).T)
        m["b2_r"] = np.ascontiguousarray(inputs["L0_ffn_b2"].reshape(16, 128).T)
        m["bc_r"] = np.ascontiguousarray(inputs["L0_attn_bc"].reshape(16, 128).T)
        for nm, key in [("L0_Wq", "L0_attn_Wq"), ("L0_Wk", "L0_attn_Wk"),
                        ("L0_Wv", "L0_attn_Wv"), ("L0_Wg", "L0_attn_Wg"),
                        ("L0_Wc", "L0_attn_Wc"), ("L1_Wq", "L1_attn_Wq"),
                        ("L1_Wk", "L1_attn_Wk"), ("L1_Wv", "L1_attn_Wv")]:
            m[nm] = np.ascontiguousarray(inputs[key])
        m["L1_Wp"] = np.ascontiguousarray(inputs["L1_attn_Wp"].T)
        m["w1_shard"] = np.ascontiguousarray(inputs["L0_ffn_w1"][c * 1024:(c + 1) * 1024])
        m["w2_shard"] = np.ascontiguousarray(inputs["L0_ffn_w2"][c * 256:(c + 1) * 256])
        m["gate_shard"] = np.ascontiguousarray(inputs["L1_ffn_gate_w"][c * 1024:(c + 1) * 1024])
        m["up_shard"] = np.ascontiguousarray(inputs["L1_ffn_up_w"][c * 1024:(c + 1) * 1024])
        m["down_shard"] = np.ascontiguousarray(inputs["L1_ffn_down_w"][c * 256:(c + 1) * 256])
        maps.append(m)
    return maps


def kernel(**inputs):
    kb = KB()
    nc = kb.build()
    maps = _stage_inputs(inputs)
    res = run_bass_kernel_spmd(nc, maps, core_ids=list(range(NCORE)))
    out = np.empty((B, T, D), dtype=np.float32)
    for c in range(NCORE):
        b, r = c // GRP, (c % GRP) * TOK
        of = res.results[c]["out_fm"]
        out[b, r:r + TOK, :] = of.reshape(D, TOK).T
    return out
